# version 09 (final)
# speedup vs baseline: 1.2953x; 1.0550x over previous
import numpy as np
import concourse.bass as bass
import concourse.mybir as mybir
from concourse.bass_utils import run_bass_kernel_spmd
from contextlib import ExitStack

F32 = mybir.dt.float32
BF16 = mybir.dt.bfloat16
AF = mybir.ActivationFunctionType
ALU = mybir.AluOpType
AX = mybir.AxisListType

D = 2048
DC = 16
T = 1024
FF = 5504
FC = 43
E = 8
ER = 72
NCORES = 8
SCALE = float(128 ** -0.5)
RS = 0.125
EPS = 1e-5
TINY = 1e-8
BIG = 1e30

FGROUPS = [list(range(0, 9)), list(range(9, 18)), list(range(18, 27)),
           list(range(27, 35)), list(range(35, 43))]
NG = len(FGROUPS)
CGROUPS = [list(range(4 * g, 4 * g + 4)) for g in range(4)]

DEBUG_TAPS = ()


class Ctr:
    def __init__(self):
        self.v = {}

    def inc(self, name, amt=1):
        self.v[name] = self.v.get(name, 0) + amt
        return self.v[name]

    def get(self, name):
        return self.v.get(name, 0)


def build_program(taps=()):
    nc = bass.Bass("TRN2", target_bir_lowering=False)

    def din(name, shape, dt=F32):
        return nc.dram_tensor(name, shape, dt, kind="ExternalInput").ap()

    xT = din("xT", [D, T])
    xres = din("xres", [T, D])
    wqT = din("wqT", [D, D]); wkT = din("wkT", [D, D])
    wvT = din("wvT", [D, D]); woT = din("woT", [D, D])
    wg = din("wg", [D, FF]); wu = din("wu", [D, FF]); wd = din("wd", [FF, D])
    wr = din("wr", [D, E])
    acat_g = din("acat_g", [D, ER]); acat_u = din("acat_u", [D, ER])
    acat_d = din("acat_d", [FF, ER])
    bcat_g = din("bcat_g", [ER, FF]); bcat_u = din("bcat_u", [ER, FF])
    bcat_d = din("bcat_d", [ER, D])
    bqm = din("bqm", [128, DC]); bkm = din("bkm", [128, DC]); bvm = din("bvm", [128, DC])
    bo_row = din("bo_row", [1, D])
    eb_row = din("eb_row", [1, E])
    expand8 = din("expand8", [E, 64])
    ident = din("ident", [128, 128])
    ones_row = din("ones_row", [1, 128])
    statones = din("statones", [128, 1])
    maskmat = din("maskmat", [128, 8])

    out = nc.dram_tensor("out", [T, D], F32, kind="ExternalOutput").ap()
    tap_aps = {}
    if 'xn' in taps:
        tap_aps['t_xn'] = nc.dram_tensor("t_xn", [128, DC * T], BF16, kind="ExternalOutput").ap()
    if 'wexp' in taps:
        tap_aps['t_wexp'] = nc.dram_tensor("t_wexp", [64, T], F32, kind="ExternalOutput").ap()
    if 'acc' in taps:
        tap_aps['t_acc'] = nc.dram_tensor("t_acc", [T, D], F32, kind="ExternalOutput").ap()
    if 'ctx' in taps:
        tap_aps['t_ctx'] = nc.dram_tensor("t_ctx", [128, DC * T], BF16, kind="ExternalOutput").ap()

    cc_dummy_in = nc.dram_tensor("cc_dummy_in", [4, 128], F32, kind="Internal").ap()
    cc_dummy_out = nc.dram_tensor("cc_dummy_out", [32, 128], F32, kind="Internal").ap()
    cc_in = [nc.dram_tensor(f"cc_in{g}", [128, 4], F32, kind="Internal").ap()
             for g in range(4)]
    cc_out = [nc.dram_tensor(f"cc_out{g}", [NCORES * 128, 4], F32, kind="Internal").ap()
              for g in range(4)]

    c = Ctr()
    stk = ExitStack()
    with stk:
        def sb(name, shape, dt=F32, st=stk):
            return st.enter_context(nc.sbuf_tensor(name, shape, dt))

        def psum(name, shape, st, dt=F32, side=None):
            return st.enter_context(nc.psum_tensor(name, shape, dt, side=side))

        SEMS = {}

        def get_sem(name):
            if name not in SEMS:
                SEMS[name] = stk.enter_context(nc.semaphore(name))
            return SEMS[name]

        def mark(instr, sem, amt=1):
            if sem in ('DVE', 'ACT'):
                return (sem, c.get(sem))
            instr.then_inc(get_sem(sem), amt)
            return (sem, c.inc(sem, amt))

        _PASSTHRU = {'wait_ge', 'sem_inc', 'nop', 'dma_start'}

        class SerialEng:
            def __init__(self, eng, cname):
                self._eng = eng
                self._cname = cname

            def __getattr__(self, name):
                fn = getattr(self._eng, name)
                if name in _PASSTHRU or not callable(fn):
                    return fn

                def call(*a, **k):
                    prev = c.get(self._cname)
                    if prev:
                        self._eng.wait_ge(get_sem(self._cname), prev)
                    ins = fn(*a, **k)
                    ins.then_inc(get_sem(self._cname), 1)
                    c.inc(self._cname)
                    return ins
                return call

        def dmark(instr, stream):
            instr.then_inc(get_sem(stream), 16)
            return (stream, c.inc(stream, 16))

        def wt(eng, ev):
            if ev is None:
                return
            if isinstance(ev, list):
                for e in ev:
                    wt(eng, e)
                return
            sem, val = ev
            assert val <= c.get(sem), f"forward wait {sem} {val} > {c.get(sem)}"
            eng.wait_ge(get_sem(sem), val)

        sPE = get_sem('PE')
        sACT = get_sem('ACT')
        sDVE = get_sem('DVE')
        sCC = get_sem('CC')

        xnbf = sb("xnbf", [128, DC * T], BF16)
        acc = [sb(f"acc{t}", [128, D], F32) for t in range(8)]
        ident_sb = sb("ident_sb", [128, 128], F32)
        ones_row_sb = sb("ones_row_sb", [1, 128], F32)
        statones_f = sb("statones_f", [128, 1], F32)
        bqm_sb = sb("bqm_sb", [128, DC], F32)
        bkm_sb = sb("bkm_sb", [128, DC], F32)
        bvm_sb = sb("bvm_sb", [128, DC], F32)
        maskmat_sb = sb("maskmat_sb", [128, 8], F32)
        expand8_sb = sb("expand8_sb", [E, 64], F32)
        eb_row_sb = sb("eb_row_sb", [1, E], F32)

        blk = stk.enter_context(nc.Block())

        def vsec(fn):
            blk.vector(lambda v: fn(SerialEng(v, 'DVE')))

        def asec(fn):
            blk.scalar(lambda a: fn(SerialEng(a, 'ACT')))

        def xn_ap(ch, th=None):
            if th is None:
                return xnbf[:, ch * T:(ch + 1) * T]
            return xnbf[:, ch * T + 512 * th: ch * T + 512 * (th + 1)]

        def _(s):
            for ap_, buf in [(ident, ident_sb), (ones_row, ones_row_sb),
                             (bqm, bqm_sb), (bkm, bkm_sb), (bvm, bvm_sb),
                             (maskmat, maskmat_sb), (expand8, expand8_sb),
                             (eb_row, eb_row_sb), (statones, statones_f)]:
                dmark(s.dma_start(out=buf[:], in_=ap_), 'cst')
        blk.sync(_)
        consts_ready = ('cst', c.get('cst'))

        def _(s):
            wt(s, consts_ready)
            dmark(s.dma_start(out=cc_dummy_in, in_=ident_sb[0:4, :]), 'cdum')
        blk.sync(_)
        cdum_rdy = ('cdum', c.get('cdum'))


        with ExitStack() as B:
            wr_bf = sb("B_wr", [128, DC * E], BF16, B)
            acatg_bf = sb("B_acg", [128, DC * ER], BF16, B)
            acatu_bf = sb("B_acu", [128, DC * ER], BF16, B)
            acatd_bf = sb("B_acd", [128, FC * ER], BF16, B)
            zg_bf = sb("B_zg", [ER, T], BF16, B)
            zu_bf = sb("B_zu", [ER, T], BF16, B)
            zda = sb("B_zda", [ER, T], F32, B)
            zd_bf = sb("B_zd", [ER, T], BF16, B)
            w_exp = sb("B_wexp", [64, T], F32, B)
            ltok = sb("B_ltok", [128, 64], F32, B)
            sc_sb = sb("B_sc", [128, 64], F32, B)
            wtok = sb("B_wtok", [128, 64], F32, B)
            rt = [sb(f"B_rt{i}", [128, 64], F32, B) for i in range(4)]
            rcol = sb("B_rcol", [128, 32], F32, B)
            ebias_bc = sb("B_ebbc", [128, E], F32, B)
            wgu_ring = [sb(f"B2_wgu{i}", [128, D], BF16, B) for i in range(4)]
            bcg_ring = [sb(f"B2_bcg{i}", [ER, 128], BF16, B) for i in range(2)]
            bcu_ring = [sb(f"B2_bcu{i}", [ER, 128], BF16, B) for i in range(2)]


            Bright = ExitStack()
            p_lg = psum("B_lg", [E, T], Bright, side="right")
            p_zg = psum("B_zgp", [ER, T], Bright, side="right")

            with ExitStack() as A:
                xf1 = [sb(f"A_x1{i}", [128, T], F32, A) for i in range(3)]
                sqf = [sb(f"A_sq{i}", [128, T], F32, A) for i in range(2)]
                xf2 = [sb(f"A_x2{i}", [128, T], F32, A) for i in range(3)]
                rowbuf = sb("A_rows", [1, 4 * T], F32, A)
                rstd_row = sb("A_rstd", [1, T], F32, A)
                negmr_row = sb("A_negmr", [1, T], F32, A)
                tmpA = sb("A_tmp", [128, T], F32, A)

                m2 = rowbuf[:, 0:T]
                var = rowbuf[:, T:2 * T]
                sd = rowbuf[:, 2 * T:3 * T]
                mean_row = rowbuf[:, 3 * T:4 * T]

                with ExitStack() as A1ps:
                    p_mean = psum("A_pmean", [1, T], A1ps)
                    p_msq = psum("A_pmsq", [1, T], A1ps)

                    x1_done = [None, None, None]
                    sq_done = [None, None]
                    for ch in range(DC):
                        sl = ch % 3
                        sl2 = ch % 2

                        def _(g, ch=ch, sl=sl):
                            wt(g, x1_done[sl])
                            dmark(g.dma_start(out=xf1[sl][:],
                                              in_=xT[ch * 128:(ch + 1) * 128, :]), f'xa{sl}')
                        blk.gpsimd(_)
                        x1_ld = (f'xa{sl}', c.get(f'xa{sl}'))

                        def _(a, sl=sl, sl2=sl2, x1_ld=x1_ld):
                            wt(a, x1_ld)
                            wt(a, sq_done[sl2])
                            a.activation(out=sqf[sl2][:], in_=xf1[sl][:], func=AF.Square)
                        asec(_)
                        sq_rdy = ('ACT', c.get('ACT'))

                        def _(t, ch=ch, sl=sl, sl2=sl2, x1_ld=x1_ld, sq_rdy=sq_rdy):
                            wt(t, x1_ld)
                            if ch == 0:
                                wt(t, consts_ready)
                            t.matmul(p_mean[:, 0:512], statones_f[:], xf1[sl][:, 0:512],
                                     start=(ch == 0), stop=(ch == DC - 1))
                            mm = t.matmul(p_mean[:, 512:1024], statones_f[:],
                                          xf1[sl][:, 512:1024],
                                          start=(ch == 0), stop=(ch == DC - 1))
                            xmark = mark(mm, 'PE')
                            x1_done[sl] = [xmark, sq_rdy]
                            wt(t, sq_rdy)
                            t.matmul(p_msq[:, 0:512], statones_f[:], sqf[sl2][:, 0:512],
                                     start=(ch == 0), stop=(ch == DC - 1))
                            mm = t.matmul(p_msq[:, 512:1024], statones_f[:],
                                          sqf[sl2][:, 512:1024],
                                          start=(ch == 0), stop=(ch == DC - 1))
                            sq_done[sl2] = mark(mm, 'PE')
                        blk.tensor(_)
                    stats_done = ('PE', c.get('PE'))

                    def _(v):
                        wt(v, stats_done)
                        v.tensor_copy(mean_row, p_mean[:])
                        v.tensor_tensor(m2, mean_row, mean_row, op=ALU.mult)
                        v.tensor_tensor(var, p_msq[:], m2, op=ALU.subtract)
                        v.tensor_scalar(var, var, EPS, None, op0=ALU.add)
                    vsec(_)
                    var_rdy = ('DVE', c.get('DVE'))

                    def _(a, var_rdy=var_rdy):
                        wt(a, var_rdy)
                        a.activation(out=sd, in_=var, func=AF.Sqrt)
                    asec(_)
                    sd_rdy = ('ACT', c.get('ACT'))

                    def _(v, sd_rdy=sd_rdy):
                        wt(v, sd_rdy)
                        v.reciprocal(rstd_row[:], sd)
                        v.scalar_tensor_tensor(negmr_row[:], mean_row, -1.0, rstd_row[:],
                                               op0=ALU.mult, op1=ALU.mult)
                    vsec(_)
                    rows_rdy = ('DVE', c.get('DVE'))

                def _(g):
                    dmark(g.dma_start(out=wr_bf[:].rearrange("p (c e) -> p c e", c=DC),
                                  in_=wr.rearrange("(c p) e -> p c e", p=128)), 'binit')
                    dmark(g.dma_start(out=acatg_bf[:].rearrange("p (c r) -> p c r", c=DC),
                                  in_=acat_g.rearrange("(c p) r -> p c r", p=128)), 'binit')
                    dmark(g.dma_start(out=acatu_bf[:].rearrange("p (c r) -> p c r", c=DC),
                                  in_=acat_u.rearrange("(c p) r -> p c r", p=128)), 'binit')
                    dmark(g.dma_start(out=acatd_bf[:].rearrange("p (c r) -> p c r", c=FC),
                                  in_=acat_d.rearrange("(c p) r -> p c r", p=128)), 'binit')
                blk.gpsimd(_)
                bweights_rdy = ('binit', c.get('binit'))
                def _(g):
                    wt(g, cdum_rdy)
                    ins = g.collective_compute(
                        "AllGather", ALU.bypass,
                        replica_groups=[list(range(NCORES))],
                        ins=[cc_dummy_in.opt()], outs=[cc_dummy_out.opt()])
                    mark(ins, 'CC')
                blk.gpsimd(_)
                with ExitStack() as A2ps:
                    p_rb = psum("A_prb", [128, T], A2ps)
                    p_nb = psum("A_pnb", [128, T], A2ps)

                    def _(t, rows_rdy=rows_rdy):
                        wt(t, rows_rdy)
                        t.matmul(p_rb[:, 0:512], ones_row_sb[:], rstd_row[:, 0:512],
                                 start=True, stop=True)
                        t.matmul(p_rb[:, 512:1024], ones_row_sb[:], rstd_row[:, 512:1024],
                                 start=True, stop=True)
                        t.matmul(p_nb[:, 0:512], ones_row_sb[:], negmr_row[:, 0:512],
                                 start=True, stop=True)
                        mm = t.matmul(p_nb[:, 512:1024], ones_row_sb[:],
                                      negmr_row[:, 512:1024], start=True, stop=True)
                        mark(mm, 'PE')
                    blk.tensor(_)
                    bcast_rdy = ('PE', c.get('PE'))

                    x2_done = [None, None, None]
                    xn_ready = [None] * DC
                    for ch in range(DC):
                        sl = ch % 3

                        def _(s, ch=ch, sl=sl):
                            wt(s, x2_done[sl])
                            dmark(s.dma_start(out=xf2[sl][:],
                                              in_=xT[ch * 128:(ch + 1) * 128, :]), f'xf{sl}')
                        blk.sync(_)
                        x2_ld = (f'xf{sl}', c.get(f'xf{sl}'))

                        def _(v, ch=ch, sl=sl, x2_ld=x2_ld):
                            wt(v, x2_ld)
                            if ch == 0:
                                wt(v, bcast_rdy)
                            v.tensor_tensor(tmpA[:], xf2[sl][:], p_rb[:], op=ALU.mult)
                            v.tensor_tensor(xn_ap(ch), tmpA[:], p_nb[:], op=ALU.add)
                        vsec(_)
                        x2_done[sl] = ('DVE', c.get('DVE'))
                        xn_ready[ch] = x2_done[sl]

                        def _(t, ch=ch):
                            wt(t, xn_ready[ch])
                            if ch == 0:
                                wt(t, bweights_rdy)
                            t.matmul(p_lg[:, 0:512], wr_bf[:, ch * E:(ch + 1) * E],
                                     xn_ap(ch, 0), start=(ch == 0), stop=(ch == DC - 1))
                            mm = t.matmul(p_lg[:, 512:1024], wr_bf[:, ch * E:(ch + 1) * E],
                                          xn_ap(ch, 1), start=(ch == 0), stop=(ch == DC - 1))
                            if ch == DC - 1:
                                mark(mm, 'PE')
                            t.matmul(p_zg[:, 0:512], acatg_bf[:, ch * ER:(ch + 1) * ER],
                                     xn_ap(ch, 0), start=(ch == 0), stop=(ch == DC - 1))
                            mm = t.matmul(p_zg[:, 512:1024],
                                          acatg_bf[:, ch * ER:(ch + 1) * ER],
                                          xn_ap(ch, 1), start=(ch == 0), stop=(ch == DC - 1))
                            if ch == DC - 1:
                                mark(mm, 'PE')
                        blk.tensor(_)
                    lg_mm = ('PE', c.get('PE') - 1)
                    zg_mm = ('PE', c.get('PE'))
            xn_all = ('DVE', c.get('DVE'))

            if 't_xn' in tap_aps:
                def _(s):
                    wt(s, xn_all)
                    dmark(s.dma_start(out=tap_aps['t_xn'], in_=xnbf[:]), 'tap')
                blk.sync(_)

            with ExitStack() as B0:
                lrow = sb("B0_lrow", [E, T], F32, B0)
                wT_sb = sb("B0_wT", [E, T], F32, B0)
                p_tp = psum("B0_tp", [128, 128], B0)
                p_wt = psum("B0_wt", [128, 128], B0)
                p_we = psum("B0_we", [64, T], B0)

                def _(t):
                    wt(t, consts_ready)
                    mm = t.matmul(p_tp[:, 0:E], ones_row_sb[:], eb_row_sb[:], start=True, stop=True)
                    mark(mm, 'PE')
                blk.tensor(_)
                ebmm_rdy = ('PE', c.get('PE'))

                def _(a, lg_mm=lg_mm, ebmm_rdy=ebmm_rdy):
                    wt(a, lg_mm)
                    a.activation(out=lrow[:], in_=p_lg[:], func=AF.Copy)
                    wt(a, ebmm_rdy)
                    a.activation(out=ebias_bc[:], in_=p_tp[:, 0:E], func=AF.Copy)
                asec(_)
                lrow_rdy = ('ACT', c.get('ACT') - 1)
                ebbc_rdy = ('ACT', c.get('ACT'))
                prev_evac = [None, None]
                for tc in range(8):
                    ptile = [p_tp, p_wt][tc % 2]

                    def _(t, tc=tc, ptile=ptile, pe=prev_evac[tc % 2]):
                        if tc == 0:
                            wt(t, lrow_rdy)
                            wt(t, ebbc_rdy)
                        wt(t, pe)
                        mm = t.transpose(ptile[:, 0:E],
                                         lrow[:, tc * 128:(tc + 1) * 128],
                                         ident_sb[0:E, 0:E])
                        mark(mm, 'PE')
                    blk.tensor(_)
                    ltp_rdy = ('PE', c.get('PE'))

                    def _(v, tc=tc, ptile=ptile, ltp_rdy=ltp_rdy):
                        wt(v, ltp_rdy)
                        v.tensor_copy(ltok[:, tc * E:(tc + 1) * E], ptile[:, 0:E])
                    vsec(_)
                    prev_evac[tc % 2] = ('DVE', c.get('DVE'))
                ltok_rdy = ('DVE', c.get('DVE'))

                def _(a, ltok_rdy=ltok_rdy):
                    wt(a, ltok_rdy)
                    a.activation(out=sc_sb[:], in_=ltok[:], func=AF.Sigmoid)
                asec(_)
                sc_rdy = ('ACT', c.get('ACT'))

                def _(v, sc_rdy=sc_rdy):
                    def seg(ap):
                        return ap.rearrange("p (t e) -> p t e", e=E)
                    def col(ap):
                        return ap.rearrange("p (t e) -> p t e", e=1)
                    def colb(ap):
                        return col(ap).broadcast_to((128, 8, E))
                    bia, mask1, bia2, wnum = (r[:] for r in rt)
                    m1 = rcol[:, 0:8]; m2r = rcol[:, 8:16]
                    den = rcol[:, 16:24]; winv = rcol[:, 24:32]
                    ebb = ebias_bc[:].rearrange("p (o e) -> p o e", o=1).broadcast_to((128, 8, E))
                    wt(v, sc_rdy)
                    v.tensor_tensor(seg(bia), seg(sc_sb[:]), ebb, op=ALU.add)
                    v.tensor_reduce(col(m1), seg(bia), axis=AX.X, op=ALU.max)
                    v.tensor_tensor(seg(mask1), seg(bia), colb(m1), op=ALU.is_equal)
                    v.scalar_tensor_tensor(bia2, mask1, -BIG, bia, op0=ALU.mult, op1=ALU.add)
                    v.tensor_reduce(col(m2r), seg(bia2), axis=AX.X, op=ALU.max)
                    v.tensor_tensor(seg(bia2), seg(bia2), colb(m2r), op=ALU.is_equal)
                    v.tensor_tensor(mask1, mask1, bia2, op=ALU.add)
                    v.tensor_tensor(wnum, mask1, sc_sb[:], op=ALU.mult)
                    v.tensor_reduce(col(den), seg(wnum), axis=AX.X, op=ALU.add)
                    v.tensor_scalar(den, den, TINY, None, op0=ALU.add)
                    v.reciprocal(winv, den)
                    v.tensor_tensor(seg(wtok[:]), seg(wnum), colb(winv), op=ALU.mult)
                vsec(_)
                w_rdy = ('DVE', c.get('DVE'))

                prev_wevac = [prev_evac[1], prev_evac[0]]
                for tc in range(8):
                    ptile = [p_wt, p_tp][tc % 2]

                    def _(t, tc=tc, ptile=ptile, w_rdy=w_rdy, pe=prev_wevac[tc % 2]):
                        if tc == 0:
                            wt(t, w_rdy)
                        wt(t, pe)
                        mm = t.transpose(ptile[0:E, :],
                                         wtok[:, tc * E:(tc + 1) * E], ident_sb[:])
                        mark(mm, 'PE')
                    blk.tensor(_)
                    wtp_rdy = ('PE', c.get('PE'))

                    def _(v, tc=tc, ptile=ptile, wtp_rdy=wtp_rdy):
                        wt(v, wtp_rdy)
                        v.tensor_copy(wT_sb[:, tc * 128:(tc + 1) * 128], ptile[0:E, :])
                    vsec(_)
                    prev_wevac[tc % 2] = ('DVE', c.get('DVE'))
                wT_rdy = ('DVE', c.get('DVE'))

                def _(t, wT_rdy=wT_rdy):
                    wt(t, wT_rdy)
                    t.matmul(p_we[:, 0:512], expand8_sb[:], wT_sb[:, 0:512], start=True, stop=True)
                    mm = t.matmul(p_we[:, 512:1024], expand8_sb[:], wT_sb[:, 512:1024],
                                  start=True, stop=True)
                    mark(mm, 'PE')
                blk.tensor(_)
                we_mm = ('PE', c.get('PE'))

                def _(a, we_mm=we_mm):
                    wt(a, we_mm)
                    a.activation(out=w_exp[:], in_=p_we[:], func=AF.Copy)
                asec(_)
                wexp_rdy = ('ACT', c.get('ACT'))

            if 't_wexp' in tap_aps:
                def _(s):
                    wt(s, wexp_rdy)
                    dmark(s.dma_start(out=tap_aps['t_wexp'], in_=w_exp[:]), 'tap')
                blk.sync(_)

            with ExitStack() as B1:
                z_sb = sb("B1_zsb", [ER, T], F32, B1)
                z_marks = []

                def _(a, zg_mm=zg_mm):
                    wt(a, zg_mm)
                    a.activation(out=z_sb[:], in_=p_zg[:], func=AF.Copy, scale=RS)
                asec(_)
                zgs_rdy = ('ACT', c.get('ACT'))

                def _(v, zgs_rdy=zgs_rdy):
                    wt(v, zgs_rdy)
                    wt(v, wexp_rdy)
                    v.tensor_tensor(zg_bf[0:64, :], z_sb[0:64, :], w_exp[:], op=ALU.mult)
                vsec(_)
                z_marks.append(('DVE', c.get('DVE')))

                def _(a):
                    a.activation(out=zg_bf[64:ER, :], in_=z_sb[64:ER, :], func=AF.Copy)
                asec(_)
                z_marks.append(('ACT', c.get('ACT')))
                zg_consumed = list(z_marks)

                Bright.close()

                with ExitStack() as B1b:
                    p_zu = psum("B1_zu", [ER, T], B1b)

                    def _(t):
                        wt(t, xn_all)
                        mm = None
                        for ch in range(DC):
                            t.matmul(p_zu[:, 0:512], acatu_bf[:, ch * ER:(ch + 1) * ER],
                                     xn_ap(ch, 0), start=(ch == 0), stop=(ch == DC - 1))
                            mm = t.matmul(p_zu[:, 512:1024],
                                          acatu_bf[:, ch * ER:(ch + 1) * ER],
                                          xn_ap(ch, 1), start=(ch == 0), stop=(ch == DC - 1))
                        mark(mm, 'PE')
                    blk.tensor(_)
                    zu_mm = ('PE', c.get('PE'))

                    def _(a, zu_mm=zu_mm, prev=zg_consumed):
                        wt(a, zu_mm)
                        wt(a, prev)
                        a.activation(out=z_sb[:], in_=p_zu[:], func=AF.Copy, scale=RS)
                    asec(_)
                    zus_rdy = ('ACT', c.get('ACT'))

                    def _(v, zus_rdy=zus_rdy):
                        wt(v, zus_rdy)
                        v.tensor_tensor(zu_bf[0:64, :], z_sb[0:64, :], w_exp[:], op=ALU.mult)
                    vsec(_)
                    z_marks.append(('DVE', c.get('DVE')))

                    def _(a):
                        a.activation(out=zu_bf[64:ER, :], in_=z_sb[64:ER, :], func=AF.Copy)
                    asec(_)
                    z_marks.append(('ACT', c.get('ACT')))
            z_rdy = z_marks

            hsil = sb("B2_hsil", [128, 1024], F32, B)
            bcd_bf = sb("B2_bcd", [ER, D], BF16, B)
            wd_bufs = [sb(f"B2_wd{i}", [128, D], BF16, B) for i in range(9)]
            hid_bufs = [sb(f"B2_hid{i}", [128, T], BF16, B) for i in range(9)]


            wgu_done = [None] * 4
            bc_done = [None] * 4
            gu_lds = {}
            wd_lds = {}
            down_done = [None] * NG
            acc_evac = [[None] * 8 for _ in range(NG)]

            def emit_gu_load(j):
                sl = j % 2
                evs = []
                for proj, src in ((0, wg), (1, wu)):
                    def _(g, j=j, src=src, sl=sl, proj=proj):
                        wt(g, wgu_done[sl * 2 + proj])
                        ev = dmark(g.dma_start(out=wgu_ring[sl * 2 + proj][:]
                                               .rearrange("p (cf f) -> p cf f", cf=DC),
                                               in_=src[:, j * 128:(j + 1) * 128]
                                               .rearrange("(cf p) f -> p cf f", p=128)),
                                   f'wgu{sl * 2 + proj}')
                        evs.append(ev)
                    blk.gpsimd(_)
                for proj, src, ring in ((0, bcat_g, bcg_ring), (1, bcat_u, bcu_ring)):
                    def _(g, j=j, src=src, ring=ring, sl=sl, proj=proj):
                        wt(g, bc_done[sl * 2 + proj])
                        ev = dmark(g.dma_start(out=ring[sl][:],
                                               in_=src[:, j * 128:(j + 1) * 128]),
                                   f'bcr{sl * 2 + proj}')
                        evs.append(ev)
                    blk.gpsimd(_)
                gu_lds[j] = evs

            def emit_wd_load(gi, jj, j):
                def _(g, gi=gi, jj=jj, j=j):
                    if gi > 0:
                        wt(g, down_done[gi - 1])
                    wd_lds[j] = dmark(g.dma_start(out=wd_bufs[jj][:],
                                                  in_=wd[j * 128:(j + 1) * 128, :]),
                                      f'wd{jj}')
                blk.gpsimd(_)

            FLAT = [j for grp in FGROUPS for j in grp]

            for j in FLAT[:2]:
                emit_gu_load(j)

            def _(g):
                wt(g, z_rdy)
                wt(g, ('PE', c.get('PE')))
                dmark(g.dma_start(out=bcd_bf[:], in_=bcat_d), 'bcd')
            blk.gpsimd(_)
            bcd_rdy = ('bcd', c.get('bcd'))

            pending_zd = []
            gate_evac = {}
            hid_evac = {}
            hsil_free = [None, None]
            zd_seg_rdy = [None] * NG
            zdbf_rdy = None
            dp_last = [None, None]

            for gi, grp in enumerate(FGROUPS):
                with ExitStack() as B2s1:
                    gp = [psum(f"B2_gp{gi}_{i}", [128, 512], B2s1, side="right")
                          for i in range(2)]
                    up = [psum(f"B2_up{gi}_{i}", [128, 512], B2s1, side="right")
                          for i in range(2)]
                    p_zd = psum(f"B2_zd{gi}", [ER, T], B2s1)

                    for jj, j in enumerate(grp):
                        k = FLAT.index(j)
                        if j not in wd_lds:
                            emit_wd_load(gi, jj, j)

                        sl = j % 2
                        wslot_g = wgu_ring[sl * 2]
                        wslot_u = wgu_ring[sl * 2 + 1]

                        if j == 0:
                            def _(t, wslot_g=wslot_g, wslot_u=wslot_u):
                                wt(t, gu_lds[0])
                                wt(t, xn_all)
                                wt(t, lrow_rdy)
                                wt(t, zgs_rdy)
                                for ch in range(DC):
                                    lt = wslot_g[:, ch * 128:(ch + 1) * 128]
                                    t.matmul(gp[0][:], lt, xn_ap(ch, 0),
                                             start=(ch == 0), stop=False)
                                    t.matmul(gp[1][:], lt, xn_ap(ch, 1),
                                             start=(ch == 0), stop=False)
                                for ch in range(DC):
                                    lt = wslot_u[:, ch * 128:(ch + 1) * 128]
                                    t.matmul(up[0][:], lt, xn_ap(ch, 0),
                                             start=(ch == 0), stop=False)
                                    mm = t.matmul(up[1][:], lt, xn_ap(ch, 1),
                                                  start=(ch == 0), stop=False)
                                mark(mm, 'PE')
                            blk.tensor(_)

                            def _(t, sl=sl):
                                wt(t, z_rdy)
                                t.matmul(gp[0][:], bcg_ring[sl][:], zg_bf[:, 0:512],
                                         start=False, stop=True)
                                mm = t.matmul(gp[1][:], bcg_ring[sl][:], zg_bf[:, 512:1024],
                                              start=False, stop=True)
                                gmark = mark(mm, 'PE')
                                wgu_done[sl * 2] = gmark
                                bc_done[sl * 2] = gmark
                                t.matmul(up[0][:], bcu_ring[sl][:], zu_bf[:, 0:512],
                                         start=False, stop=True)
                                mm = t.matmul(up[1][:], bcu_ring[sl][:], zu_bf[:, 512:1024],
                                              start=False, stop=True)
                                umark = mark(mm, 'PE')
                                wgu_done[sl * 2 + 1] = umark
                                bc_done[sl * 2 + 1] = umark
                            blk.tensor(_)
                        else:
                            def _(t, j=j, jj=jj, gi=gi, sl=sl,
                                  wslot_g=wslot_g, wslot_u=wslot_u, grp=grp):
                                wt(t, gu_lds[j])
                                if j - 1 in gate_evac:
                                    wt(t, gate_evac[j - 1])
                                if jj == 0 and gi > 0:
                                    wt(t, [acc_evac[gi - 1][tc2] for tc2 in (0, 2, 4, 6)])
                                mm = None
                                for ch in range(DC):
                                    lt = wslot_g[:, ch * 128:(ch + 1) * 128]
                                    t.matmul(gp[0][:], lt, xn_ap(ch, 0), start=(ch == 0), stop=False)
                                    t.matmul(gp[1][:], lt, xn_ap(ch, 1), start=(ch == 0), stop=False)
                                t.matmul(gp[0][:], bcg_ring[sl][:], zg_bf[:, 0:512],
                                         start=False, stop=True)
                                mm = t.matmul(gp[1][:], bcg_ring[sl][:], zg_bf[:, 512:1024],
                                              start=False, stop=True)
                                gmark = mark(mm, 'PE')
                                wgu_done[sl * 2] = gmark
                                bc_done[sl * 2] = gmark
                                if j - 1 in hid_evac:
                                    wt(t, hid_evac[j - 1])
                                for ch in range(DC):
                                    lt = wslot_u[:, ch * 128:(ch + 1) * 128]
                                    t.matmul(up[0][:], lt, xn_ap(ch, 0), start=(ch == 0), stop=False)
                                    t.matmul(up[1][:], lt, xn_ap(ch, 1), start=(ch == 0), stop=False)
                                t.matmul(up[0][:], bcu_ring[sl][:], zu_bf[:, 0:512],
                                         start=False, stop=True)
                                mm = t.matmul(up[1][:], bcu_ring[sl][:], zu_bf[:, 512:1024],
                                              start=False, stop=True)
                                umark = mark(mm, 'PE')
                                wgu_done[sl * 2 + 1] = umark
                                bc_done[sl * 2 + 1] = umark
                            blk.tensor(_)
                        up_mm = ('PE', c.get('PE'))
                        gate_mm = ('PE', c.get('PE') - 1)

                        for th in range(2):
                            def _(a, th=th, gate_mm=gate_mm):
                                wt(a, gate_mm)
                                wt(a, hsil_free[th])
                                a.activation(out=hsil[:, th * 512:(th + 1) * 512],
                                             in_=gp[th][:], func=AF.Silu)
                            asec(_)
                            sil_rdy = ('ACT', c.get('ACT'))
                            if th == 1:
                                gate_evac[j] = sil_rdy

                            def _(v, j=j, jj=jj, gi=gi, th=th, sil_rdy=sil_rdy, up_mm=up_mm):
                                wt(v, sil_rdy)
                                wt(v, up_mm)
                                if gi > 0 and th == 0:
                                    wt(v, down_done[gi - 1])
                                ins = v.tensor_tensor(
                                    hid_bufs[jj][:, th * 512:(th + 1) * 512],
                                    hsil[:, th * 512:(th + 1) * 512], up[th][:], op=ALU.mult)
                                hsil_free[th] = mark(ins, 'DVE')
                            vsec(_)
                            if th == 1:
                                hid_evac[j] = ('DVE', c.get('DVE'))

                        def emit_zd(j=j, jj=jj, gi=gi, grp=grp):
                            def _(t, j=j, jj=jj, gi=gi, grp=grp):
                                wt(t, hid_evac[j])
                                if j == grp[0] and gi > 0:
                                    wt(t, zd_seg_rdy[gi - 1])
                                t.matmul(p_zd[:, 0:512], acatd_bf[:, j * ER:(j + 1) * ER],
                                         hid_bufs[jj][:, 0:512],
                                         start=(j == grp[0]), stop=(j == grp[-1]))
                                mm = t.matmul(p_zd[:, 512:1024],
                                              acatd_bf[:, j * ER:(j + 1) * ER],
                                              hid_bufs[jj][:, 512:1024],
                                              start=(j == grp[0]), stop=(j == grp[-1]))
                                mark(mm, 'PE')
                            blk.tensor(_)
                        pending_zd.append(emit_zd)
                        if len(pending_zd) > 1:
                            pending_zd.pop(0)()
                        if k + 2 < FC and FLAT[k + 2] not in gu_lds:
                            emit_gu_load(FLAT[k + 2])
                    while pending_zd:
                        pending_zd.pop(0)()
                    zd_mm = ('PE', c.get('PE'))

                    if gi == 0:
                        def _(a, zd_mm=zd_mm):
                            wt(a, zd_mm)
                            a.activation(out=zda[:], in_=p_zd[:], func=AF.Copy)
                        asec(_)
                        zd_seg_rdy[gi] = ('ACT', c.get('ACT'))
                    else:
                        def _(v, zd_mm=zd_mm):
                            wt(v, zd_mm)
                            ins = v.tensor_tensor(zda[:], p_zd[:], zda[:], op=ALU.add)
                            mark(ins, 'DVE')
                        vsec(_)
                        zd_seg_rdy[gi] = ('DVE', c.get('DVE'))

                    if gi == NG - 1:
                        def _(v, gi=gi):
                            wt(v, zd_seg_rdy[gi])
                            ins = v.scalar_tensor_tensor(zd_bf[0:64, :], zda[0:64, :], RS,
                                                         w_exp[:], op0=ALU.mult, op1=ALU.mult)
                            mark(ins, 'DVE')
                        vsec(_)
                        zm1 = ('DVE', c.get('DVE'))

                        def _(a, gi=gi):
                            wt(a, zd_seg_rdy[gi])
                            a.activation(out=zd_bf[64:ER, :], in_=zda[64:ER, :], func=AF.Copy,
                                         scale=RS)
                        asec(_)
                        zdbf_rdy = [zm1, ('ACT', c.get('ACT'))]

                with ExitStack() as B2s2:
                    dp = [psum(f"B2_dp{gi}_0", [128, D], B2s2, side="right"),
                          psum(f"B2_dp{gi}_1", [128, D], B2s2)]
                    for tc in range(8):
                        sl = tc % 2

                        def _(t, gi=gi, grp=grp, tc=tc, sl=sl):
                            wt(t, dp_last[sl])
                            if tc == 0:
                                wt(t, gate_evac[grp[-1]])
                                wt(t, hid_evac[grp[-1]])
                            elif tc == 1:
                                wt(t, zd_seg_rdy[gi])
                            if gi == NG - 1:
                                wt(t, zdbf_rdy)
                                wt(t, bcd_rdy)
                            wt(t, [wd_lds[j2] for j2 in grp])
                            mm = None
                            for jj, j in enumerate(grp):
                                lt = hid_bufs[jj][:, tc * 128:(tc + 1) * 128]
                                for m in range(4):
                                    st_ = (jj == 0)
                                    sp_ = (jj == len(grp) - 1) and (gi != NG - 1)
                                    mm = t.matmul(dp[sl][:, m * 512:(m + 1) * 512], lt,
                                                  wd_bufs[jj][:, m * 512:(m + 1) * 512],
                                                  start=st_, stop=sp_)
                            if gi == NG - 1:
                                for m in range(4):
                                    mm = t.matmul(dp[sl][:, m * 512:(m + 1) * 512],
                                                  zd_bf[:, tc * 128:(tc + 1) * 128],
                                                  bcd_bf[:, m * 512:(m + 1) * 512],
                                                  start=False, stop=True)
                            mark(mm, 'PE')
                        blk.tensor(_)
                        dmk = ('PE', c.get('PE'))
                        if tc == 7:
                            down_done[gi] = dmk

                        if gi == 0:
                            def _(a, tc=tc, sl=sl, dmk=dmk):
                                wt(a, dmk)
                                a.activation(out=acc[tc][:], in_=dp[sl][:],
                                             func=AF.Copy)
                            asec(_)
                            ev = ('ACT', c.get('ACT'))
                        else:
                            def _(v, tc=tc, sl=sl, dmk=dmk):
                                wt(v, dmk)
                                ins = v.tensor_tensor(acc[tc][:], dp[sl][:], acc[tc][:],
                                                      op=ALU.add)
                                mark(ins, 'DVE')
                            vsec(_)
                            ev = ('DVE', c.get('DVE'))
                        acc_evac[gi][tc] = ev
                        dp_last[sl] = ev
            acc_rdy = [acc_evac[NG - 1][tc] for tc in range(8)]

        if 't_acc' in tap_aps:
            def _(s):
                wt(s, acc_rdy)
                for tc in range(8):
                    dmark(s.dma_start(out=tap_aps['t_acc'][tc * 128:(tc + 1) * 128, :],
                                      in_=acc[tc][:]), 'tap')
            blk.sync(_)

        ctx_bf = sb("ctx_bf", [128, DC * T], BF16)
        wo_ring = [sb(f"wo_ring{i}", [128, 1024], BF16) for i in range(4)]
        with ExitStack() as C:
            ones_fp = sb("C_ones", [128, T], F32, C)
            kvs = sb("C_kvs", [128, 4 * T], F32, C)
            qkv_ring = [sb(f"C_qkv{i}", [128, D], BF16, C) for i in range(5)]
            ek = sb("C_ek", [128, T], F32, C)
            rk = sb("C_rk", [128, T], F32, C)
            phik = sb("C_phik", [128, T], F32, C)
            phiq2 = [sb(f"C_phiq{i}", [128, T], F32, C) for i in range(3)]
            vsb = sb("C_vsb", [128, T], F32, C)
            contrib = sb("C_contrib", [128, 4], F32, C)
            gath = sb("C_gath", [128, 32], F32, C)
            carry_sb = sb("C_carry", [128, 4], F32, C)
            carry_tmp = sb("C_carrytmp", [128, 4], F32, C)

            p_kp = [psum(f"C_kp{i}", [128, T], C) for i in range(2)]
            p_vp = psum("C_vp", [128, T], C)

            def _(v):
                ins = v.memset(ones_fp[:], 1.0)
                mark(ins, 'DVE')
            vsec(_)
            ones_rdy = ('DVE', c.get('DVE'))

            ring_free = [None] * 5
            use_ld = {}

            load_plan = []
            for grp in CGROUPS:
                for i in range(4):
                    load_plan.append((('k', grp[i]), wkT, grp[i]))
                    load_plan.append((('v', grp[i]), wvT, grp[i]))
                for i in range(4):
                    load_plan.append((('q', grp[i]), wqT, grp[i]))
            next_load = [0]

            def pump_load():
                if next_load[0] >= len(load_plan):
                    return
                key, src, col = load_plan[next_load[0]]
                slot = next_load[0] % 5
                next_load[0] += 1

                def _(g, key=key, src=src, col=col, slot=slot):
                    wt(g, ring_free[slot])
                    use_ld[key] = (slot, dmark(
                        g.dma_start(out=qkv_ring[slot][:]
                                    .rearrange("p (cf f) -> p cf f", cf=DC),
                                    in_=src[:, col * 128:(col + 1) * 128]
                                    .rearrange("(cf p) f -> p cf f", p=128)), f'qkv{slot}'))
                blk.gpsimd(_)

            phi_stt_prev = None
            u_prev = None
            ctx_done_prev = [None, None, None]
            ctx_marks = {}
            v_evac_prev = None
            kp_free = [None, None]
            kvs_done = [None] * 4
            carry_users = None

            def _(g):
                wt(g, down_done[NG - 1])
                wt(g, acc_rdy)
            blk.gpsimd(_)

            for _i in range(5):
                pump_load()

            wo_pre = {}

            def emit_wo_load(q, dp):
                slot = dp % 4

                def _(g2, q=q, dp=dp, slot=slot):
                    wt(g2, wo_free[slot])
                    wo_pre[(q, dp)] = dmark(g2.dma_start(
                        out=wo_ring[slot][:].rearrange("p (c e) -> p c e", c=2),
                        in_=woT[dp * 256:(dp + 1) * 256, q * 512:(q + 1) * 512]
                        .rearrange("(c p) e -> p c e", p=128)), f'wo{slot}')
                blk.gpsimd(_)

            wo_free = [None] * 4

            for g, grp in enumerate(CGROUPS):
                if g == 3:
                    for dp_ in range(4):
                        emit_wo_load(0, dp_)
                for gi, ch in enumerate(grp):
                    kslot, k_ld = use_ld[('k', ch)]
                    vslot, v_ld = use_ld[('v', ch)]
                    pk = p_kp[gi % 2]

                    def _(t, ch=ch, gi=gi, kslot=kslot, k_ld=k_ld, pk=pk, g=g):
                        wt(t, k_ld)
                        wt(t, kp_free[gi % 2])
                        if g == 0 and gi == 0:
                            wt(t, xn_all)
                        mm = None
                        for d in range(DC):
                            lt = qkv_ring[kslot][:, d * 128:(d + 1) * 128]
                            t.matmul(pk[:, 0:512], lt, xn_ap(d, 0),
                                     start=(d == 0), stop=(d == DC - 1))
                            mm = t.matmul(pk[:, 512:1024], lt, xn_ap(d, 1),
                                          start=(d == 0), stop=(d == DC - 1))
                        ring_free[kslot] = mark(mm, 'PE')
                    blk.tensor(_)
                    pump_load()
                    k_mm = ('PE', c.get('PE'))

                    def _(a, ch=ch, pk=pk, k_mm=k_mm, u_prev=u_prev):
                        wt(a, k_mm)
                        wt(a, u_prev)
                        a.activation(out=ek[:], in_=pk[:], func=AF.Exp,
                                     bias=bkm_sb[:, ch:ch + 1])
                        a.activation(out=rk[:], in_=pk[:], func=AF.Relu,
                                     bias=bkm_sb[:, ch:ch + 1])
                    asec(_)
                    phi_act = ('ACT', c.get('ACT'))
                    kp_free[gi % 2] = phi_act

                    def _(v, phi_act=phi_act, u_prev=u_prev):
                        wt(v, phi_act)
                        wt(v, u_prev)
                        ins = v.scalar_tensor_tensor(phik[:], ek[:], 1.0, rk[:],
                                                     op0=ALU.min, op1=ALU.add)
                        mark(ins, 'DVE')
                    vsec(_)
                    phi_stt_prev = ('DVE', c.get('DVE'))

                    def _(t, ch=ch, vslot=vslot, v_ld=v_ld, v_evac_prev=v_evac_prev):
                        wt(t, v_ld)
                        wt(t, v_evac_prev)
                        mm = None
                        for d in range(DC):
                            lt = qkv_ring[vslot][:, d * 128:(d + 1) * 128]
                            t.matmul(p_vp[:, 0:512], lt, xn_ap(d, 0),
                                     start=(d == 0), stop=(d == DC - 1))
                            mm = t.matmul(p_vp[:, 512:1024], lt, xn_ap(d, 1),
                                          start=(d == 0), stop=(d == DC - 1))
                        ring_free[vslot] = mark(mm, 'PE')
                    blk.tensor(_)
                    pump_load()
                    v_mm = ('PE', c.get('PE'))

                    def _(v, ch=ch, v_mm=v_mm, u_prev=u_prev):
                        wt(v, v_mm)
                        wt(v, u_prev)
                        ins = v.tensor_scalar(vsb[:], p_vp[:], bvm_sb[:, ch:ch + 1], None,
                                              op0=ALU.add)
                        mark(ins, 'DVE')
                    vsec(_)
                    v_evac_prev = ('DVE', c.get('DVE'))

                    def _(v, gi=gi, v_evac_prev=v_evac_prev, g=g):
                        wt(v, v_evac_prev)
                        v.tensor_tensor(ek[:], phik[:], vsb[:], op=ALU.mult)
                        if g == 0 and gi == 0:
                            wt(v, ones_rdy)
                        wt(v, kvs_done[gi])
                        v.tensor_tensor_scan(kvs[:, gi * T:(gi + 1) * T],
                                             ones_fp[:], ek[:], 0.0,
                                             op0=ALU.mult, op1=ALU.add)
                    vsec(_)
                    u_prev = ('DVE', c.get('DVE'))
                scan_done = ('DVE', c.get('DVE'))

                def _(v, g=g, carry_users=carry_users):
                    wt(v, carry_users)
                    for gi in range(4):
                        v.tensor_copy(contrib[:, gi:gi + 1],
                                      kvs[:, gi * T + T - 1: gi * T + T])
                vsec(_)
                contrib_rdy = ('DVE', c.get('DVE'))

                def _(s, g=g, contrib_rdy=contrib_rdy):
                    wt(s, contrib_rdy)
                    dmark(s.dma_start(out=cc_in[g], in_=contrib[:]), 'ccin')
                blk.sync(_)
                ccin_rdy = ('ccin', c.get('ccin'))

                def _(gp_, g=g, ccin_rdy=ccin_rdy):
                    wt(gp_, ccin_rdy)
                    ins = gp_.collective_compute(
                        "AllGather", ALU.bypass,
                        replica_groups=[list(range(NCORES))],
                        ins=[cc_in[g].opt()], outs=[cc_out[g].opt()])
                    mark(ins, 'CC')
                blk.gpsimd(_)
                cc_rdy = ('CC', c.get('CC'))

                def _(s, g=g, cc_rdy=cc_rdy, prev_car=carry_users):
                    wt(s, cc_rdy)
                    wt(s, prev_car)
                    dmark(s.dma_start(
                        out=gath[:].rearrange("p (r cc) -> p r cc", cc=4),
                        in_=cc_out[g].rearrange("(r p) cc -> p r cc", p=128)), 'gath')
                blk.sync(_)
                gath_rdy = ('gath', c.get('gath'))

                q_mm_ev = [None] * 4
                phiq_stt = [None] * 4

                def emit_q(gi):
                    ch = grp[gi]
                    qslot, q_ld = use_ld[('q', ch)]
                    pq = p_kp[gi % 2]

                    def _(t, ch=ch, gi=gi, qslot=qslot, q_ld=q_ld, pq=pq):
                        wt(t, q_ld)
                        wt(t, kp_free[gi % 2])
                        mm = None
                        for d in range(DC):
                            lt = qkv_ring[qslot][:, d * 128:(d + 1) * 128]
                            t.matmul(pq[:, 0:512], lt, xn_ap(d, 0),
                                     start=(d == 0), stop=(d == DC - 1))
                            mm = t.matmul(pq[:, 512:1024], lt, xn_ap(d, 1),
                                          start=(d == 0), stop=(d == DC - 1))
                        ring_free[qslot] = mark(mm, 'PE')
                    blk.tensor(_)
                    pump_load()
                    q_mm_ev[gi] = ('PE', c.get('PE'))

                def emit_phi_q(gi):
                    nonlocal phi_stt_prev, ctx_done_prev
                    ch = grp[gi]
                    pq = p_kp[gi % 2]

                    def _(a, gi=gi, ch=ch, pq=pq, q_mm=q_mm_ev[gi],
                          phi_stt_prev=phi_stt_prev, scan_grp=u_prev):
                        wt(a, q_mm)
                        wt(a, phi_stt_prev)
                        if gi == 0:
                            wt(a, scan_grp)
                        a.activation(out=ek[:], in_=pq[:], func=AF.Exp,
                                     bias=bqm_sb[:, ch:ch + 1])
                        a.activation(out=rk[:], in_=pq[:], func=AF.Relu,
                                     bias=bqm_sb[:, ch:ch + 1])
                    asec(_)
                    phiq_act = ('ACT', c.get('ACT'))
                    kp_free[gi % 2] = phiq_act

                    def _(v, gi=gi, phiq_act=phiq_act):
                        wt(v, phiq_act)
                        wt(v, ctx_done_prev[gi % 3])
                        v.scalar_tensor_tensor(phiq2[gi % 3][:], ek[:], 1.0, rk[:],
                                               op0=ALU.min, op1=ALU.add)
                    vsec(_)
                    phi_stt_prev = ('DVE', c.get('DVE'))
                    phiq_stt[gi] = phi_stt_prev

                def emit_ctx(gi):
                    ch = grp[gi]

                    def _(v, ch=ch, gi=gi, carry_rdy=carry_rdy):
                        wt(v, carry_rdy)
                        v.scalar_tensor_tensor(
                            ctx_bf[:, ch * T:(ch + 1) * T],
                            kvs[:, gi * T:(gi + 1) * T],
                            carry_sb[:, gi:gi + 1],
                            phiq2[gi % 3][:], op0=ALU.add, op1=ALU.mult)
                    vsec(_)
                    kvs_done[gi] = ('DVE', c.get('DVE'))
                    ctx_done_prev[gi % 3] = kvs_done[gi]
                    ctx_marks[ch] = kvs_done[gi]

                carry_rdy = None
                emit_q(0)
                emit_phi_q(0)
                emit_q(1)
                emit_phi_q(1)
                emit_q(2)
                emit_phi_q(2)
                emit_q(3)

                def _(v, gath_rdy=gath_rdy):
                    wt(v, gath_rdy)
                    v.tensor_scalar(carry_tmp[:], gath[:, 0:4], maskmat_sb[:, 0:1],
                                    None, op0=ALU.mult)
                    for r in range(1, NCORES):
                        dst = carry_sb if r % 2 == 1 else carry_tmp
                        srcb = carry_tmp if r % 2 == 1 else carry_sb
                        v.scalar_tensor_tensor(dst[:], gath[:, 4 * r:4 * r + 4],
                                               maskmat_sb[:, r:r + 1], srcb[:],
                                               op0=ALU.mult, op1=ALU.add)
                vsec(_)
                carry_rdy = ('DVE', c.get('DVE'))

                emit_ctx(0)
                emit_phi_q(3)
                emit_ctx(1)
                emit_ctx(2)
                emit_ctx(3)
                u_prev = kvs_done[3]
                carry_users = kvs_done[3]
            attn_tail = [kp_free[0], kp_free[1], v_evac_prev]
        ctx_rdy = ('DVE', c.get('DVE'))

        if 't_ctx' in tap_aps:
            def _(s):
                wt(s, ctx_rdy)
                dmark(s.dma_start(out=tap_aps['t_ctx'], in_=ctx_bf[:]), 'tap')
            blk.sync(_)

        with ExitStack() as C3:
            bo_adj = sb("C3_boadj", [1, D], BF16, C3)
            onesb = sb("C3_onesb", [1, 128], BF16, C3)
            xr_ring = [sb(f"C3_xr{i}", [128, 512], F32, C3) for i in range(8)]
            outb = [sb(f"C3_ob{i}", [128, 512], F32, C3) for i in range(8)]
            gsil = [sb(f"C3_gs{i}", [128, 512], F32, C3) for i in range(4)]
            o2_b = sb("C3_o2", [128, 512], F32, C3)
            o1_arr = [sb(f"C3_o1_{i}", [128, 512], F32, C3) for i in range(8)]

            boscope = ExitStack()
            bo_row_sb = sb("C3_borow", [1, D], F32, boscope)

            def _(s):
                wt(s, ctx_rdy)
                dmark(s.dma_start(out=bo_row_sb[:], in_=bo_row), 'borow')
            blk.sync(_)
            bo_ld = ('borow', c.get('borow'))

            def _(g, bo_ld=bo_ld):
                wt(g, consts_ready)
                wt(g, ctx_rdy)
                dmark(g.dma_start(out=onesb[:], in_=ones_row_sb[:]), 'onesb')
            blk.gpsimd(_)
            onesb_rdy = ('onesb', c.get('onesb'))

            def _(a, bo_ld=bo_ld):
                wt(a, bo_ld)
                a.activation(out=bo_adj[:], in_=bo_row_sb[:], func=AF.Copy,
                             scale=1.0 / SCALE)
            asec(_)
            boadj_rdy = ('ACT', c.get('ACT'))
            boscope.close()

            with ExitStack() as C3b:
                sp = [psum(f"C3_sp{i}", [128, 512], C3b) for i in range(8)]
                xr_free = [None] * 8
                ob_free = [None] * 8
                o1_done = [None] * 8
                gsil_free = [None] * 4
                sp_free = [None] * 8
                xr_lds = [None] * 8

                for q in range(4):
                    for tc in range(8):
                        def _(s, q=q, tc=tc):
                            wt(s, xr_free[tc])
                            dmark(s.dma_start(out=xr_ring[tc][:],
                                              in_=xres[tc * 128:(tc + 1) * 128,
                                                       q * 512:(q + 1) * 512]), f'xr{tc}')
                        blk.sync(_)
                        xr_lds[tc] = (f'xr{tc}', c.get(f'xr{tc}'))

                    gs_rdy = [None] * 8

                    def emit_gsil(q, tc):
                        def _(a, q=q, tc=tc):
                            wt(a, acc_rdy[tc])
                            wt(a, gsil_free[tc % 4])
                            a.activation(out=gsil[tc % 4][:],
                                         in_=acc[tc][:, q * 512:(q + 1) * 512],
                                         func=AF.Silu)
                        asec(_)
                        gs_rdy[tc] = ('ACT', c.get('ACT'))

                    for tc in range(4):
                        emit_gsil(q, tc)

                    q_mms = [None] * 8
                    for dp in range(8):
                        slot = dp % 4
                        if (q, dp) not in wo_pre:
                            emit_wo_load(q, dp)
                        wo_ld = wo_pre[(q, dp)]

                        def _(t, q=q, dp=dp, slot=slot, wo_ld=wo_ld):
                            wt(t, wo_ld)
                            if q == 0:
                                wt(t, ctx_marks[2 * dp])
                                wt(t, ctx_marks[2 * dp + 1])
                                if dp == 0:
                                    wt(t, attn_tail)
                            mm = None
                            for dd in range(2):
                                d = 2 * dp + dd
                                for tc in range(8):
                                    if d == 0:
                                        wt(t, sp_free[tc])
                                    mm = t.matmul(
                                        sp[tc][:],
                                        ctx_bf[:, d * T + tc * 128: d * T + (tc + 1) * 128],
                                        wo_ring[slot][:, dd * 512:(dd + 1) * 512],
                                        start=(d == 0), stop=False)
                            mark(mm, 'PE')
                            wo_free[slot] = ('PE', c.get('PE'))
                        blk.tensor(_)

                    def _(t, q=q):
                        wt(t, onesb_rdy)
                        wt(t, boadj_rdy)
                        for tc in range(8):
                            mm = t.matmul(sp[tc][:], onesb[:],
                                          bo_adj[:, q * 512:(q + 1) * 512],
                                          start=False, stop=True)
                            q_mms[tc] = mark(mm, 'PE')
                    blk.tensor(_)

                    for tc in range(8):
                        def _(v, q=q, tc=tc):
                            wt(v, q_mms[tc])
                            wt(v, o1_done[tc])
                            v.tensor_scalar(o1_arr[tc][:], sp[tc][:], SCALE, None,
                                            op0=ALU.mult)
                            sp_free[tc] = ('DVE', c.get('DVE'))
                        vsec(_)

                    for tc in range(8):
                        slot = tc
                        if tc >= 4:
                            emit_gsil(q, tc)

                        def _(v, q=q, tc=tc, slot=slot, gs_rdy=gs_rdy[tc],
                              xr_ld=xr_lds[tc]):
                            wt(v, gs_rdy)
                            v.tensor_tensor(o2_b[:], o1_arr[tc][:], gsil[tc % 4][:],
                                            op=ALU.mult)
                            o1_done[tc] = ('DVE', c.get('DVE'))
                            gsil_free[tc % 4] = o1_done[tc]
                            wt(v, xr_ld)
                            wt(v, ob_free[slot])
                            v.tensor_tensor(outb[slot][:], o2_b[:], xr_ring[tc][:],
                                            op=ALU.add)
                        vsec(_)
                        comb_rdy = ('DVE', c.get('DVE'))
                        xr_free[tc] = comb_rdy

                        def _(s, q=q, tc=tc, slot=slot, comb_rdy=comb_rdy):
                            wt(s, comb_rdy)
                            dmark(s.dma_start(out=out[tc * 128:(tc + 1) * 128,
                                                      q * 512:(q + 1) * 512],
                                              in_=outb[slot][:]), f'out{slot}')
                        blk.sync(_)
                        ob_free[slot] = (f'out{slot}', c.get(f'out{slot}'))

        def _(s):
            for sl in range(4):
                wt(s, (f'out{sl}', c.get(f'out{sl}')))
            if c.get('tap'):
                wt(s, ('tap', c.get('tap')))
        blk.sync(_)

    return nc


def _prep(inp):
    f32 = np.float32

    def a(x):
        return np.ascontiguousarray(np.asarray(x, dtype=f32))

    x = a(inp['L0_attn_x'])
    Wq = a(inp['L0_attn_Wq']); Wk = a(inp['L0_attn_Wk'])
    Wv = a(inp['L0_attn_Wv']); Wo = a(inp['L0_attn_Wo'])
    bq = a(inp['L0_attn_bq']); bk = a(inp['L0_attn_bk'])
    bv = a(inp['L0_attn_bv']); bo = a(inp['L0_attn_bo'])
    Wr = a(inp['L0_ffn_W_router'])
    Wg = a(inp['L0_ffn_W_gate']); Wu = a(inp['L0_ffn_W_up'])
    Wd = a(inp['L0_ffn_W_down'])
    sAg = a(inp['L0_ffn_sA_gate']); sBg = a(inp['L0_ffn_sB_gate'])
    sAu = a(inp['L0_ffn_sA_up']); sBu = a(inp['L0_ffn_sB_up'])
    sAd = a(inp['L0_ffn_sA_down']); sBd = a(inp['L0_ffn_sB_down'])
    Ag = a(inp['L0_ffn_A_gate']); Bg = a(inp['L0_ffn_B_gate'])
    Au = a(inp['L0_ffn_A_up']); Bu = a(inp['L0_ffn_B_up'])
    Ad = a(inp['L0_ffn_A_down']); Bd = a(inp['L0_ffn_B_down'])
    eb = a(inp['L0_ffn_expert_bias'])

    com = dict(
        wqT=np.ascontiguousarray(Wq.T), wkT=np.ascontiguousarray(Wk.T),
        wvT=np.ascontiguousarray(Wv.T), woT=np.ascontiguousarray(Wo.T),
        wg=Wg, wu=Wu, wd=Wd, wr=Wr,
        acat_g=np.ascontiguousarray(np.concatenate(
            [Ag.transpose(1, 0, 2).reshape(D, 64), sAg], axis=1)),
        acat_u=np.ascontiguousarray(np.concatenate(
            [Au.transpose(1, 0, 2).reshape(D, 64), sAu], axis=1)),
        acat_d=np.ascontiguousarray(np.concatenate(
            [Ad.transpose(1, 0, 2).reshape(FF, 64), sAd], axis=1)),
        bcat_g=np.ascontiguousarray(np.concatenate(
            [Bg.reshape(64, FF), sBg], axis=0)),
        bcat_u=np.ascontiguousarray(np.concatenate(
            [Bu.reshape(64, FF), sBu], axis=0)),
        bcat_d=np.ascontiguousarray(np.concatenate(
            [Bd.reshape(64, D), sBd], axis=0)),
        bqm=np.ascontiguousarray(bq.reshape(DC, 128).T),
        bkm=np.ascontiguousarray(bk.reshape(DC, 128).T),
        bvm=np.ascontiguousarray(bv.reshape(DC, 128).T),
        bo_row=bo.reshape(1, D),
        eb_row=eb.reshape(1, E),
        expand8=np.ascontiguousarray(
            np.eye(E, dtype=f32).repeat(8, axis=1)),
        ident=np.eye(128, dtype=f32),
        ones_row=np.ones((1, 128), dtype=f32),
        statones=np.full((128, 1), 1.0 / D, dtype=f32),
    )
    in_maps = []
    for core in range(NCORES):
        b = core // 2
        h = core % 2
        xs = x[b, h * T:(h + 1) * T, :]
        mm = np.zeros((128, NCORES), dtype=f32)
        if h == 1:
            mm[:, core - 1] = 1.0
        m = dict(com)
        m['xT'] = np.ascontiguousarray(xs.T)
        m['xres'] = np.ascontiguousarray(xs)
        m['maskmat'] = mm
        in_maps.append(m)
    return in_maps


def kernel(**inp):
    in_maps = _prep(inp)
    nc = build_program(taps=DEBUG_TAPS)
    res = run_bass_kernel_spmd(nc, in_maps, core_ids=list(range(NCORES)))
    out = np.empty((4, 2048, D), dtype=np.float32)
    for core in range(NCORES):
        b = core // 2
        h = core % 2
        out[b, h * T:(h + 1) * T, :] = res.results[core]['out']
    return out
